# version 04 (final)
# baseline (speedup 1.0000x reference)
import math
import numpy as np
import contextlib
import concourse.bass as bass
import concourse.mybir as mybir
from concourse.bass_utils import run_bass_kernel_spmd

F32 = mybir.dt.float32
BF16 = mybir.dt.bfloat16
AF = mybir.ActivationFunctionType
ALU = mybir.AluOpType
AX = mybir.AxisListType

B, T, C = 4, 2048, 2048
H, KVH, HD = 16, 4, 128
NB = 16
TBLK = 8
NTOK = TBLK * 128
QS = [[0, 3, 4, 7, 8, 11, 12, 15],
      [1, 2, 5, 6, 9, 10, 13, 14]]
GMAX = [2 * q + 1 for q in range(8)]
EPS = 1e-5
QSCALE = 1.0 / math.sqrt(HD)
_UPTO = "full"
_NDCH = 4
_NHC = 64
_NHEADS = 16
_NQSL = 8
_DO_AV = True
_NO_MASK = False
_NO_TR = False
_TR_MAXKB = 99
_NO_EV = False
_ATTN_DUMP = None


def _slopes():
    start = 2.0 ** (-2.0 ** (-(np.log2(H) - 3.0)))
    return np.array([start ** (i + 1) for i in range(H)], np.float64)


DMA_STREAMS = {"XL": 3, "XR": 2, "XO": 2, "W": 4, "KV": 2, "KB": 4}


class Plan:

    def __init__(self, nc, stack):
        self.nc = nc
        self.ops = []
        self.cnt = {}
        self.sems = {}
        self.last_wait = {}
        self.dma_idx = {}
        names = ["SPE", "SDV", "SAC", "SGP", "SCC"]
        for st_, n in DMA_STREAMS.items():
            names += [f"{st_}{i}" for i in range(n)]
        for name in names:
            self.sems[name] = stack.enter_context(nc.semaphore(name))
            self.cnt[name] = 0

    SERIAL = {"dv": "SDV", "ac": "SAC"}

    def emit(self, eng, fn, waits=(), inc=None, step=1):
        waits = list(waits)
        if eng in self.SERIAL:
            sem = self.SERIAL[eng]
            waits.append((sem, self.cnt[sem]))
            if inc is None:
                inc = sem
                step = 1
            assert inc == sem
        ws = []
        for (s, v) in waits:
            if v is None or v <= 0:
                continue
            k = (eng, s)
            if self.last_wait.get(k, -1) >= v:
                continue
            self.last_wait[k] = v
            ws.append((s, v))
        if inc is not None:
            self.cnt[inc] += step
        self.ops.append((eng, fn, ws, (inc, step) if inc else None))
        return self.cnt[inc] if inc else None

    def dma(self, eng, stream, fn, waits=()):
        n = DMA_STREAMS[stream]
        i = self.dma_idx.get(stream, 0)
        self.dma_idx[stream] = i + 1
        sem = f"{stream}{i % n}"
        w = list(waits)
        if i >= n:
            w.append((sem, 16 * (i // n)))
        self.emit(eng, fn, waits=w, inc=sem, step=16)
        return (sem, 16 * (i // n + 1))

    def dma_barrier(self, stream):
        n = DMA_STREAMS[stream]
        i = self.dma_idx.get(stream, 0)
        toks = []
        for k in range(n):
            cnt = (i - k + n - 1) // n
            if cnt > 0:
                toks.append((f"{stream}{k}", 16 * cnt))
        return toks

    def replay(self, eng_name, eng):
        for (e, fn, ws, inc) in self.ops:
            if e != eng_name:
                continue
            for (s, v) in ws:
                eng.wait_ge(self.sems[s], v)
            ins = fn(eng)
            if inc is not None:
                ins.then_inc(self.sems[inc[0]], inc[1])


class Builder:
    def __init__(self, detect_races=True):
        self.nc = bass.Bass("TRN2", target_bir_lowering=False,
                            detect_race_conditions=detect_races)
        nc = self.nc
        self.X = nc.dram_tensor("X", [NTOK, C], F32, kind="ExternalInput").ap()
        self.WG = nc.dram_tensor("WG", [64, 128, 16, 128], F32, kind="ExternalInput").ap()
        self.WU = nc.dram_tensor("WU", [64, 128, 16, 128], F32, kind="ExternalInput").ap()
        self.WD = nc.dram_tensor("WD", [4, 64, 128, 512], F32, kind="ExternalInput").ap()
        self.WQ = nc.dram_tensor("WQ", [16, 128, 16, 128], F32, kind="ExternalInput").ap()
        self.WKVK = nc.dram_tensor("WKVK", [4, 128, 16, 128], F32, kind="ExternalInput").ap()
        self.WKVV = nc.dram_tensor("WKVV", [16, 128, 512], F32, kind="ExternalInput").ap()
        self.WO = nc.dram_tensor("WO", [4, 16, 128, 512], F32, kind="ExternalInput").ap()
        self.BO = nc.dram_tensor("BO", [1, C], F32, kind="ExternalInput").ap()
        self.MASKT = nc.dram_tensor("MASKT", [128, 8 * 256], F32, kind="ExternalInput").ap()
        self.EVT = nc.dram_tensor("EVT", [128, H * 128], F32, kind="ExternalInput").ap()
        self.OUT = nc.dram_tensor("OUT", [NTOK, C], F32, kind="ExternalOutput").ap()
        self.XSPA = nc.dram_tensor("XSPA", [NTOK, C], F32).ap()
        self.XSPB = nc.dram_tensor("XSPB", [NTOK, C], F32).ap()
        self.KCI = nc.dram_tensor("KCI", [KVH * 128, NTOK], BF16).ap()
        self.VCI = nc.dram_tensor("VCI", [NTOK, KVH * 128], BF16).ap()
        self.KCO = nc.dram_tensor("KCO", [2 * KVH * 128, NTOK], BF16).ap()
        self.VCO = nc.dram_tensor("VCO", [2 * NTOK, KVH * 128], BF16).ap()
        self.st = contextlib.ExitStack()
        self.pl = Plan(nc, self.st)

    _uid = 0

    def sb(self, name, shape, dt, **kw):
        if name is None:
            Builder._uid += 1
            name = f"t{Builder._uid}"
        cm = self.nc.sbuf_tensor(name, shape, dt, **kw)
        return cm.__enter__(), cm

    def ps(self, shape, dt=F32):
        Builder._uid += 1
        cm = self.nc.psum_tensor(f"q{Builder._uid}", shape, dt)
        return cm.__enter__(), cm

    @staticmethod
    def close(*cms):
        for cm in cms:
            cm.__exit__(None, None, None)

    def setup(self):
        pl = self.pl
        self.ident, _ = self.sb("ident", [128, 128], BF16, side="left")
        self.ones_bf, _ = self.sb("ones_bf", [128, 1], BF16, side="left")
        self.ones_f, _ = self.sb("ones_f", [1, 128], F32, side="left")
        self.negm, _ = self.sb("negm", [128, 1], F32, side="left")
        self.stat, _ = self.sb("stat", [128, 8], F32, side="left")
        self.recip_row, _ = self.sb("recip_row", [1, 1024], F32, side="left")
        self.tmp128, _ = self.sb("tmp128", [128, 128], BF16, side="left")
        self.eps_sb, _ = self.sb("eps_sb", [128, 1], F32, side="left")
        self.xn, _ = self.sb("xn", [128, 16 * 1024], BF16, side="left")

        t_ms = pl.emit("gp", lambda e: e.memset(self.tmp128[:, :], 1.0),
                       inc="SGP")
        pl.emit("gp", lambda e: e.affine_select(
            out=self.ident[:, :], in_=self.tmp128[:, :], pattern=[[-1, 128]],
            compare_op=ALU.is_equal, fill=0.0, base=0, channel_multiplier=1),
            waits=[("SGP", t_ms)], inc="SGP")
        self.tok_ident = pl.cnt["SGP"]
        pl.emit("dv", lambda e: e.memset(self.ones_bf[:, :], 1.0))
        pl.emit("dv", lambda e: e.memset(self.eps_sb[:, :], EPS))
        pl.emit("dv", lambda e: e.memset(self.ones_f[:, :], 1.0), inc="SDV")
        self.tok_ones = pl.cnt["SDV"]
        self.ln_guard = 0

    def emit_ln(self, xsrc, src_toks):
        pl = self.pl
        xn_sb = self.xn
        x_tm, cm_x = self.sb(None, [128, TBLK * C], F32, side="right")
        stg, cm1 = self.sb(None, [128, 2 * C], BF16)
        scr, cm2 = self.sb(None, [128, C], F32)
        ptr, cm3 = self.ps([128, 2048])
        stat = self.stat
        apply_toks = []
        evac_toks = []
        tr_toks = []
        prev_var = 0
        for i in range(TBLK):
            xsl = x_tm[:, i * C:(i + 1) * C]
            w_ld = [("SDV", self.ln_guard)] + pl.dma_barrier("XO")
            if src_toks[i] is not None:
                w_ld.append(src_toks[i])
            t_ld = pl.dma("sy", "XL", lambda e, i=i, xsl=xsl: e.dma_start(
                out=xsl, in_=xsrc[i * 128:(i + 1) * 128, :]), waits=w_ld)
            wait_x = [t_ld]
            pl.emit("dv", lambda e, xsl=xsl: e.reduce_sum(
                out=stat[:, 0:1], in_=xsl, axis=AX.X), waits=wait_x)
            t_sq = pl.emit("ac", lambda e, xsl=xsl: e.activation(
                out=scr[:, :], in_=xsl, func=AF.Square,
                accum_out=stat[:, 1:2]),
                waits=list(wait_x) + [("SDV", prev_var)], inc="SAC")
            pl.emit("dv", lambda e: e.tensor_scalar(
                out=stat[:, 2:3], in0=stat[:, 0:1], scalar1=1.0 / C,
                scalar2=None, op0=ALU.mult))
            pl.emit("dv", lambda e: e.tensor_tensor(
                out=stat[:, 3:4], in0=stat[:, 2:3], in1=stat[:, 2:3],
                op=ALU.mult))
            t_var = pl.emit("dv", lambda e: e.tensor_scalar(
                out=stat[:, 4:5], in0=stat[:, 1:2], scalar1=1.0 / C,
                scalar2=stat[:, 3:4], op0=ALU.mult, op1=ALU.subtract),
                waits=[("SAC", t_sq)], inc="SDV")
            prev_var = t_var
            t_sd = pl.emit("ac", lambda e: e.activation(
                out=stat[:, 5:6], in_=stat[:, 4:5], func=AF.Sqrt,
                bias=self.eps_sb[:, 0:1]),
                waits=[("SDV", t_var)], inc="SAC")
            pl.emit("dv", lambda e: e.reciprocal(
                out=stat[:, 6:7], in_=stat[:, 5:6]), waits=[("SAC", t_sd)])
            par = i % 2
            ssl = stg[:, par * C:(par + 1) * C]
            wtr = [("SPE", tr_toks[i - 2])] if i >= 2 else []
            t_app = pl.emit("dv", lambda e, xsl=xsl, ssl=ssl: e.tensor_scalar(
                out=ssl, in0=xsl, scalar1=stat[:, 2:3], scalar2=stat[:, 6:7],
                op0=ALU.subtract, op1=ALU.mult), waits=wtr, inc="SDV")
            apply_toks.append(t_app)
            wev = [("SDV", evac_toks[i - 1])] if i >= 1 else []
            for ci in range(16):
                pl.emit("pe", lambda e, ssl=ssl, ci=ci: e.matmul(
                    ptr[:, ci * 128:(ci + 1) * 128],
                    lhsT=ssl[:, ci * 128:(ci + 1) * 128],
                    rhs=self.ident[:, :], start=True, stop=True),
                    waits=[("SDV", t_app), ("SGP", self.tok_ident)] + wev)
            tt = pl.emit("pe", lambda e: e.drain(), inc="SPE")
            tr_toks.append(tt)
            dst = xn_sb.ap().rearrange("p (c t f) -> p c t f", c=16, f=128)[:, :, i, :]
            ev = pl.emit("dv", lambda e, dst=dst: e.tensor_copy(
                out=dst, in_=ptr.ap().rearrange("p (c f) -> p c f", f=128)),
                waits=[("SPE", tt)], inc="SDV")
            evac_toks.append(ev)
        self.close(cm3, cm2, cm1, cm_x)
        self.ln_guard = apply_toks[-1]
        return evac_toks[-1]

    def emit_ffn(self, xsrc, src_toks, xdst, final, stage_a_only=False):
        pl = self.pl
        xn = self.xn
        t_xn = self.emit_ln(xsrc, src_toks)

        g, cm_g = self.sb(None, [128, 64 * 1024], BF16)
        wgb, cm1 = self.sb(None, [128, 2 * 2048], BF16)
        wub, cm2 = self.sb(None, [128, 2 * 2048], BF16)
        sil, cm3 = self.sb(None, [128, 1024], F32)
        fa0, cmp0 = self.ps([128, 2048])
        fa1, cmp1 = self.ps([128, 2048])
        fa = [fa0, fa1]

        pe_toks = [0] * 64
        dv_toks = [0] * 64
        for hc in range(64):
            par = hc % 2
            wdma = [("SPE", pe_toks[hc - 2])] if hc >= 2 else \
                [("SDV", t_xn)]
            tg = pl.dma("gp", "W", lambda e, hc=hc, par=par: e.dma_start(
                out=wgb[:, par * 2048:(par + 1) * 2048]
                .rearrange("p (c f) -> p c f", f=128), in_=self.WG[hc]),
                waits=wdma)
            tu = pl.dma("gp", "W", lambda e, hc=hc, par=par: e.dma_start(
                out=wub[:, par * 2048:(par + 1) * 2048]
                .rearrange("p (c f) -> p c f", f=128), in_=self.WU[hc]))
            pw = [tg, tu, ("SDV", t_xn)]
            if hc >= 2:
                pw.append(("SDV", dv_toks[hc - 2]))
            tok = None
            for u, wsb in ((0, wgb), (1, wub)):
                for s in range(2):
                    dst = fa[par][:, u * 1024 + s * 512: u * 1024 + s * 512 + 512]
                    for ci in range(16):
                        last = (u == 1 and s == 1 and ci == 15)
                        tok = pl.emit("pe", lambda e, dst=dst, wsb=wsb,
                                      par=par, ci=ci, s=s: e.matmul(
                                          dst,
                                          lhsT=wsb[:, par * 2048 + ci * 128:
                                                   par * 2048 + ci * 128 + 128],
                                          rhs=xn[:, ci * 1024 + s * 512:
                                                 ci * 1024 + s * 512 + 512],
                                          start=(ci == 0), stop=(ci == 15)),
                                      waits=pw if (u == 0 and s == 0 and ci == 0)
                                      else (), inc="SPE" if last else None)
            tok = pl.emit("pe", lambda e: e.drain(), inc="SPE")
            pe_toks[hc] = tok
            wsil = [("SPE", tok)]
            if hc >= 1:
                wsil.append(("SDV", dv_toks[hc - 1]))
            t_sil = pl.emit("ac", lambda e, par=par: e.activation(
                out=sil[:, :], in_=fa[par][:, 0:1024], func=AF.Silu),
                waits=wsil, inc="SAC")
            dv_toks[hc] = pl.emit("dv", lambda e, hc=hc, par=par:
                                  e.scalar_tensor_tensor(
                                      out=g[:, hc * 1024:(hc + 1) * 1024],
                                      in0=fa[par][:, 1024:2048], scalar=0.5,
                                      in1=sil[:, :], op0=ALU.mult,
                                      op1=ALU.mult),
                                  waits=[("SAC", t_sil)], inc="SDV")
        t_sA_pe = pe_toks[63]
        t_sA_dv = dv_toks[63]
        self.close(cmp1, cmp0, cm3, cm2, cm1)

        if stage_a_only:
            stg2, cm_s2 = self.sb(None, [128, 2048], F32)
            for k in range(8):
                t_c = pl.emit("dv", lambda e, k=k: e.tensor_copy(
                    out=stg2[:, :], in_=g[:, k * 2048:(k + 1) * 2048]),
                    waits=[("SDV", t_sA_dv)])
                t_o = pl.dma("sy", "XO", lambda e, k=k: e.dma_start(
                    out=self.OUT[k * 128:(k + 1) * 128, :], in_=stg2[:, :]),
                    waits=[("SDV", t_c)])
                pl.emit("dv", lambda e: e.engine_nop(), waits=[t_o])
            self.close(cm_s2, cm_g)
            return None

        wdb, cm4 = self.sb(None, [128, 2 * 512], BF16)
        sgi, cm5 = self.sb(None, [128, 2 * 512], F32)
        sgo, cm6 = self.sb(None, [128, 2 * 512], F32)
        dn, dncm = [], []
        for t in range(8):
            h_, c_ = self.ps([128, 512])
            dn.append(h_)
            dncm.append(c_)
        evac_tok = {}
        wd_guard = {}
        chain_tok = {}
        tt_tok = {}
        od_tok = {}
        ndch = _NDCH
        nhc = _NHC
        for dch in range(ndch):
            for hc in range(nhc):
                par = hc % 2
                idx = dch * 64 + hc
                w = [("SPE", wd_guard[idx - 2])] if idx >= 2 else \
                    [("SPE", t_sA_pe), ("SDV", t_sA_dv)]
                t_wd = pl.dma("gp", "W", lambda e, dch=dch, hc=hc, par=par:
                              e.dma_start(
                                  out=wdb[:, par * 512:(par + 1) * 512],
                                  in_=self.WD[dch, hc]),
                              waits=w)
                for t in range(8):
                    w2 = [t_wd]
                    if hc == 0:
                        w2.append(("SDV", t_sA_dv))
                        if dch > 0:
                            w2.append(("SDV", evac_tok[(dch - 1, t)]))
                    tk = pl.emit("pe", lambda e, t=t, hc=hc, par=par: e.matmul(
                        dn[t][:, :],
                        lhsT=g[:, hc * 1024 + t * 128: hc * 1024 + t * 128 + 128],
                        rhs=wdb[:, par * 512:(par + 1) * 512],
                        start=(hc == 0), stop=(hc == nhc - 1)),
                        waits=w2, inc="SPE" if t == 7 else None)
                wd_guard[idx] = pl.cnt["SPE"]
            t_dr = pl.emit("pe", lambda e: e.drain(), inc="SPE")
            for t in range(8):
                chain_tok[(dch, t)] = t_dr
            for t in range(8):
                k = dch * 8 + t
                par = k % 2
                isl = sgi[:, par * 512:(par + 1) * 512]
                osl = sgo[:, par * 512:(par + 1) * 512]
                wrl = [("SPE", t_sA_pe), ("SDV", t_sA_dv)]
                if k >= 2:
                    wrl.append(("SDV", tt_tok[k - 2]))
                t_rl = pl.dma("sy", "XR", lambda e, t=t, dch=dch, isl=isl:
                              e.dma_start(
                                  out=isl,
                                  in_=xsrc[t * 128:(t + 1) * 128,
                                           dch * 512:(dch + 1) * 512]),
                              waits=wrl)
                wtt = [("SPE", chain_tok[(dch, t)]), t_rl]
                if k >= 2:
                    wtt.append(od_tok[k - 2])
                tt_tok[k] = pl.emit(
                    "dv", lambda e, t=t, isl=isl, osl=osl: e.tensor_tensor(
                        out=osl, in0=dn[t][:, :], in1=isl, op=ALU.add),
                    waits=wtt, inc="SDV")
                evac_tok[(dch, t)] = tt_tok[k]
                od_tok[k] = pl.dma(
                    "sy", "XO", lambda e, t=t, dch=dch, osl=osl: e.dma_start(
                        out=xdst[t * 128:(t + 1) * 128,
                                 dch * 512:(dch + 1) * 512], in_=osl),
                    waits=[("SDV", tt_tok[k])])
        self.close(cm6, cm5, cm4, *reversed(dncm))
        self.close(cm_g)
        self.ffn_guard = wd_guard[(ndch - 1) * 64 + nhc - 1]
        return [od_tok[(ndch - 1) * 8 + t] for t in range(8)]

    def emit_dump(self, sb_tensor, width_bytes, waits):
        pl = self.pl
        stg, cm = self.sb(None, [128, 2048], F32)
        for k in range(8):
            t_c = pl.emit("dv", lambda e, k=k: e.tensor_copy(
                out=stg[:, :],
                in_=sb_tensor[:, k * 2048:(k + 1) * 2048]), waits=waits)
            t_o = pl.dma("sy", "XO", lambda e, k=k: e.dma_start(
                out=self.OUT[k * 128:(k + 1) * 128, :], in_=stg[:, :]),
                waits=[("SDV", t_c)])
            pl.emit("dv", lambda e: e.engine_nop(), waits=[t_o])
        self.close(cm)

    def emit_attention(self, xsrc, src_toks, xdst):
        pl = self.pl
        xn2 = self.xn
        t_xn2 = self.emit_ln(xsrc, src_toks)

        attn_fm, cm_af = self.sb(None, [128, H * 1024], BF16)
        K_AG, cm_k = self.sb(None, [128, KVH * 2048], BF16)
        V_AG, cm_v = self.sb(None, [128, NB * 512], BF16)
        wkb, cm_wk = self.sb(None, [128, 2 * 2048], BF16)
        wvb, cm_wv = self.sb(None, [128, 2 * 512], BF16)

        kv0, cmkv0 = self.ps([128, 512])
        kv1, cmkv1 = self.ps([128, 512])
        kvp = [kv0, kv1]
        k_pe = {}
        k_ac = {}
        for kvh in range(KVH):
            par = kvh % 2
            w = [("SDV", t_xn2)]
            if kvh >= 2:
                w = [("SPE", k_pe[(kvh - 2, 1)])]
            t_dk = pl.dma("gp", "W", lambda e, kvh=kvh, par=par: e.dma_start(
                out=wkb[:, par * 2048:(par + 1) * 2048]
                .rearrange("p (c f) -> p c f", f=128), in_=self.WKVK[kvh]),
                waits=w)
            for s in range(2):
                w2 = [t_dk, ("SDV", t_xn2)]
                if kvh >= 1:
                    w2.append(("SAC", k_ac[(kvh - 1, s)]))
                tk = None
                for ci in range(16):
                    tk = pl.emit("pe", lambda e, par=par, s=s, ci=ci: e.matmul(
                        kvp[s][:, :],
                        lhsT=wkb[:, par * 2048 + ci * 128:
                                 par * 2048 + ci * 128 + 128],
                        rhs=xn2[:, ci * 1024 + s * 512:
                                ci * 1024 + s * 512 + 512],
                        start=(ci == 0), stop=(ci == 15)),
                        waits=w2 if ci == 0 else (),
                        inc="SPE" if ci == 15 else None)
                tk = pl.emit("pe", lambda e: e.drain(), inc="SPE")
                k_pe[(kvh, s)] = tk
                k_ac[(kvh, s)] = pl.emit("ac", lambda e, kvh=kvh, s=s:
                                         e.activation(
                                             out=K_AG[:, kvh * 2048 + s * 512:
                                                      kvh * 2048 + s * 512 + 512],
                                             in_=kvp[s][:, :], func=AF.Copy),
                                         waits=[("SPE", tk)], inc="SAC")
        self.close(cmkv1, cmkv0)

        vch, vcm = [], []
        for t in range(8):
            h_, c_ = self.ps([128, 512])
            vch.append(h_)
            vcm.append(c_)
        v_pe = [0] * 16
        v_ac = {}
        for ci in range(16):
            par = ci % 2
            w = [("SPE", v_pe[ci - 2])] if ci >= 2 else []
            t_dv_ = pl.dma("gp", "W", lambda e, ci=ci, par=par: e.dma_start(
                out=wvb[:, par * 512:(par + 1) * 512], in_=self.WKVV[ci]),
                waits=w)
            for t in range(8):
                w2 = [t_dv_, ("SDV", t_xn2),
                      ("SAC", k_ac[(3, 1)])]
                v_pe[ci] = pl.emit("pe", lambda e, ci=ci, par=par, t=t:
                                   e.matmul(
                                       vch[t][:, :],
                                       lhsT=xn2[:, ci * 1024 + t * 128:
                                                ci * 1024 + t * 128 + 128],
                                       rhs=wvb[:, par * 512:(par + 1) * 512],
                                       start=(ci == 0), stop=(ci == 15)),
                                   waits=w2 if t == 0 else (),
                                   inc="SPE" if t == 7 else None)
        t_vdr = pl.emit("pe", lambda e: e.drain(), inc="SPE")
        for t in range(8):
            v_ac[t] = pl.emit("ac", lambda e, t=t: e.activation(
                out=V_AG[:, t * 512:(t + 1) * 512], in_=vch[t][:, :],
                func=AF.Copy), waits=[("SPE", t_vdr)], inc="SAC")
        self.close(*reversed(vcm))
        self.close(cm_wv, cm_wk)

        for kvh in range(KVH):
            pl.dma("gp", "KV", lambda e, kvh=kvh: e.dma_start(
                out=self.KCI[kvh * 128:(kvh + 1) * 128, :],
                in_=K_AG[:, kvh * 2048: kvh * 2048 + 1024]),
                waits=[("SAC", k_ac[(kvh, 1)])])
        for t in range(8):
            pl.dma("gp", "KV", lambda e, t=t: e.dma_start(
                out=self.VCI[t * 128:(t + 1) * 128, :],
                in_=V_AG[:, t * 512:(t + 1) * 512]),
                waits=[("SAC", v_ac[t])])
        pl.emit("gp", lambda e: e.collective_compute(
            "AllGather", ALU.bypass,
            replica_groups=[[0, 1], [2, 3], [4, 5], [6, 7]],
            ins=[self.KCI], outs=[self.KCO]),
            waits=pl.dma_barrier("KV"), inc="SCC")
        t_cc = pl.emit("gp", lambda e: e.collective_compute(
            "AllGather", ALU.bypass,
            replica_groups=[[0, 1], [2, 3], [4, 5], [6, 7]],
            ins=[self.VCI], outs=[self.VCO]),
            inc="SCC")
        parity = [0 if gg in QS[0] else 1 for gg in range(NB)]
        pos = [QS[parity[gg]].index(gg) for gg in range(NB)]
        for gg in range(NB):
            r, p = parity[gg], pos[gg]
            for kvh in range(KVH):
                pl.dma("gp", "KB", lambda e, gg=gg, r=r, p=p, kvh=kvh:
                       e.dma_start(
                           out=K_AG[:, kvh * 2048 + gg * 128:
                                    kvh * 2048 + gg * 128 + 128],
                           in_=self.KCO[r * 512 + kvh * 128:
                                        r * 512 + kvh * 128 + 128,
                                        p * 128:(p + 1) * 128]),
                       waits=[("SCC", t_cc)])
            pl.dma("gp", "KB", lambda e, gg=gg, r=r, p=p: e.dma_start(
                out=V_AG[:, gg * 512:(gg + 1) * 512],
                in_=self.VCO[r * 1024 + p * 128: r * 1024 + p * 128 + 128, :]),
                waits=[("SCC", t_cc)])
        kvback = pl.dma_barrier("KB")

        if _ATTN_DUMP == "qkv":
            dmp, cmD = self.sb(None, [128, 16384], BF16)
            pl.emit("dv", lambda e: e.tensor_copy(
                out=dmp[:, 0:8192], in_=K_AG[:, :]), waits=kvback)
            t_cv = pl.emit("dv", lambda e: e.tensor_copy(
                out=dmp[:, 8192:16384], in_=V_AG[:, :]), waits=kvback)
            self.emit_dump(dmp, None, [("SDV", t_cv)])
            self.close(cmD)
            return [None] * 8

        PT, cm_pt = self.sb(None, [128, NB * 1024], BF16)
        P_sb, cm_p = self.sb(None, [128, 2048], BF16)
        Q_sb, cm_q = self.sb(None, [128, 2 * 1024], BF16)
        mask_sb, cm_m = self.sb(None, [128, 8 * 256], F32)
        ev_sb, cm_e = self.sb(None, [128, H * 128], F32)
        bc_sb, cm_bc = self.sb(None, [128, 1024], F32)
        wqb, cm_wq = self.sb(None, [128, 2 * 2048], BF16)
        S, cm_s = self.ps([128, 2048])
        QAV, cm_qav = self.ps([128, 1024])
        TRP, cm_trp = self.ps([128, 1024])

        t_mk = pl.dma("gp", "W", lambda e: e.dma_start(
            out=mask_sb[:, :], in_=self.MASKT), waits=[("SPE", v_pe[15])])
        t_ev = pl.dma("gp", "W", lambda e: e.dma_start(
            out=ev_sb[:, :], in_=self.EVT))

        bank_tok = [[], []]
        qpe = [0] * 16
        t_qav_free = 0
        t_rb_free = 0
        exp_tok = 0
        lasttr_tok = 0
        t_af = 0
        for h in range(_NHEADS):
            kvh = h % KVH
            parq = h % 2
            w = [("SPE", qpe[h - 2])] if h >= 2 else [("SPE", v_pe[15])]
            t_dq = pl.dma("gp", "W", lambda e, h=h, parq=parq: e.dma_start(
                out=wqb[:, parq * 2048:(parq + 1) * 2048]
                .rearrange("p (c f) -> p c f", f=128), in_=self.WQ[h]),
                waits=w)
            for s in range(2):
                w2 = [t_dq, ("SDV", max(t_xn2, t_qav_free)),
                      ("SAC", v_ac[7])]
                tq = None
                for ci in range(16):
                    tq = pl.emit("pe", lambda e, parq=parq, s=s, ci=ci:
                                 e.matmul(
                                     QAV[:, s * 512:(s + 1) * 512],
                                     lhsT=wqb[:, parq * 2048 + ci * 128:
                                              parq * 2048 + ci * 128 + 128],
                                     rhs=xn2[:, ci * 1024 + s * 512:
                                             ci * 1024 + s * 512 + 512],
                                     start=(ci == 0), stop=(ci == 15)),
                                 waits=w2 if ci == 0 else (),
                                 inc="SPE" if ci == 15 else None)
                tq = pl.emit("pe", lambda e: e.drain(), inc="SPE")
                pl.emit("ac", lambda e, parq=parq, s=s: e.activation(
                    out=Q_sb[:, parq * 1024 + s * 512:
                             parq * 1024 + s * 512 + 512],
                    in_=QAV[:, s * 512:(s + 1) * 512], func=AF.Copy,
                    scale=QSCALE), waits=[("SPE", tq)], inc="SAC")
            t_qevac = pl.cnt["SAC"]
            qpe[h] = pl.cnt["SPE"]

            for q in range(_NQSL):
                L = 256 * (q + 1)
                wsc = kvback + [("SAC", pl.cnt["SAC"]),
                                ("SDV", pl.cnt["SDV"])]
                nch = (L + 511) // 512
                t_s = None
                for c in range(nch):
                    c0 = c * 512
                    wd_ = min(512, L - c0)
                    t_s = pl.emit("pe", lambda e, parq=parq, q=q, kvh=kvh,
                                  c0=c0, wd_=wd_: e.matmul(
                                      S[:, c0:c0 + wd_],
                                      lhsT=Q_sb[:, parq * 1024 + q * 128:
                                                parq * 1024 + q * 128 + 128],
                                      rhs=K_AG[:, kvh * 2048 + c0:
                                               kvh * 2048 + c0 + wd_],
                                      start=True, stop=True),
                                  waits=wsc if c == 0 else ())
                t_s = pl.emit("pe", lambda e: e.drain(), inc="SPE")
                if not _NO_MASK:
                    pl.emit("dv", lambda e, q=q, L=L: e.tensor_tensor(
                        out=S[:, L - 256:L], in0=S[:, L - 256:L],
                        in1=mask_sb[:, q * 256:(q + 1) * 256], op=ALU.add),
                        waits=[("SPE", t_s), t_mk, ("SAC", exp_tok)])
                t_m = pl.emit("dv", lambda e, L=L: e.reduce_max(
                    out=self.negm[:, 0:1], in_=S[:, 0:L], axis=AX.X,
                    negate=True), waits=[("SPE", t_s), ("SAC", exp_tok)],
                    inc="SDV")
                exp_tok = pl.emit("ac", lambda e, L=L: e.activation(
                    out=P_sb[:, 0:L], in_=S[:, 0:L], func=AF.Exp,
                    bias=self.negm[:, 0:1]),
                    waits=[("SDV", t_m), ("SPE", lasttr_tok)], inc="SAC")
                nkb = min(L // 128, _TR_MAXKB)
                if _NO_TR:
                    lasttr_tok = pl.cnt["SPE"]
                    continue
                for grp in range((nkb + 3) // 4):
                    kbs_g = list(range(grp * 4, min(nkb, grp * 4 + 4)))
                    bank = grp % 2
                    wtr = [("SAC", pl.cnt["SAC"]), ("SDV", pl.cnt["SDV"])]
                    for kb in kbs_g:
                        slot = (kb % 4) + 4 * bank
                        pl.emit("pe", lambda e, kb=kb, slot=slot:
                                e.matmul(
                                    TRP[:, slot * 128:(slot + 1) * 128],
                                    lhsT=P_sb[:, kb * 128:(kb + 1) * 128],
                                    rhs=self.ident[:, :], start=True,
                                    stop=True),
                                waits=wtr if kb == kbs_g[0] else ())
                    t_t = pl.emit("pe", lambda e: e.drain(), inc="SPE")
                    lasttr_tok = t_t
                    tv_ac = tv_dv = 0
                    if _NO_EV:
                        bank_tok[bank] = []
                        continue
                    for kb in kbs_g:
                        slot = (kb % 4) + 4 * bank
                        evix = h * 128 + q * 16 + kb
                        dst = PT[:, kb * 1024 + q * 128: kb * 1024 + q * 128 + 128]
                        srcp = TRP[:, slot * 128:(slot + 1) * 128]
                        if False:
                            tv_ac = pl.emit("ac", lambda e, dst=dst, srcp=srcp,
                                            evix=evix: e.activation(
                                                out=dst, in_=srcp, func=AF.Copy,
                                                scale=ev_sb[:, evix:evix + 1]),
                                            waits=[("SPE", t_t), t_ev],
                                            inc="SAC")
                        else:
                            tv_dv = pl.emit("dv", lambda e, dst=dst, srcp=srcp,
                                            evix=evix: e.tensor_scalar(
                                                out=dst, in0=srcp,
                                                scalar1=ev_sb[:, evix:evix + 1],
                                                scalar2=None, op0=ALU.mult),
                                            waits=[("SPE", t_t), t_ev],
                                            inc="SDV")
                    bank_tok[bank] = ([("SAC", tv_ac)] if tv_ac else []) + \
                        ([("SDV", tv_dv)] if tv_dv else [])

            if not _DO_AV:
                t_af = pl.emit("dv", lambda e: e.engine_nop(), inc="SDV")
                t_qav_free = t_af
                t_rb_free = t_af
                continue
            wav = [("SDV", pl.cnt["SDV"]), ("SAC", pl.cnt["SAC"])] + kvback
            if t_qav_free:
                wav.append(("SDV", t_qav_free))
            first = True
            t_av = None
            for q5 in range(2):
                kbs = [kb for kb in range(NB) if (kb // 2) * 128 < (q5 + 1) * 512]
                for kb in kbs:
                    lo = max((kb // 2) * 128, q5 * 512)
                    hi = (q5 + 1) * 512
                    pl.emit("pe", lambda e, kb=kb, kvh=kvh, lo=lo, hi=hi,
                            kst=(kb == kbs[0]), ken=(kb == kbs[-1]): e.matmul(
                                QAV[:, lo:hi],
                                lhsT=V_AG[:, kb * 512 + kvh * 128:
                                          kb * 512 + kvh * 128 + 128],
                                rhs=PT[:, kb * 1024 + lo: kb * 1024 + hi],
                                start=kst, stop=ken),
                            waits=wav if first else ())
                    first = False
                    pl.emit("pe", lambda e, kb=kb, lo=lo, hi=hi,
                            kst=(kb == kbs[0]), ken=(kb == kbs[-1]):
                            e.matmul(
                                S[0:1, lo:hi], lhsT=self.ones_bf[:, 0:1],
                                rhs=PT[:, kb * 1024 + lo: kb * 1024 + hi],
                                start=kst, stop=ken))
            t_av = pl.emit("pe", lambda e: e.drain(), inc="SPE")
            t_rc = pl.emit("dv", lambda e: e.reciprocal(
                out=self.recip_row[0:1, :], in_=S[0:1, 0:1024]),
                waits=[("SPE", t_av), ("SDV", self.tok_ones)], inc="SDV")
            for q5 in range(2):
                pl.emit("pe", lambda e, q5=q5: e.matmul(
                    S[:, q5 * 512:(q5 + 1) * 512], lhsT=self.ones_f[0:1, :],
                    rhs=self.recip_row[0:1, q5 * 512:(q5 + 1) * 512],
                    start=True, stop=True),
                    waits=[("SDV", t_rc)])
            t_bc = pl.emit("pe", lambda e: e.drain(), inc="SPE")
            pl.emit("dv", lambda e: e.tensor_copy(
                out=bc_sb[:, :], in_=S[:, 0:1024]), waits=[("SPE", t_bc)])
            t_af = pl.emit("dv", lambda e, h=h: e.tensor_tensor(
                out=attn_fm[:, h * 1024:(h + 1) * 1024], in0=QAV[:, :],
                in1=bc_sb[:, :], op=ALU.mult), inc="SDV")
            t_qav_free = t_af
            t_rb_free = t_af
        t_attn = t_af

        if _ATTN_DUMP == "pt":
            self.emit_dump(PT, None,
                           [("SDV", pl.cnt["SDV"]), ("SAC", pl.cnt["SAC"])])
            self.close(cm_trp, cm_qav, cm_s)
            self.close(cm_wq, cm_bc, cm_e, cm_m, cm_q, cm_p, cm_pt)
            self.close(cm_v, cm_k)
            self.close(cm_af)
            return [None] * 8
        if _ATTN_DUMP == "af":
            self.emit_dump(attn_fm, None, [("SDV", t_attn)])
            self.close(cm_trp, cm_qav, cm_s)
            self.close(cm_wq, cm_bc, cm_e, cm_m, cm_q, cm_p, cm_pt)
            self.close(cm_v, cm_k)
            self.close(cm_af)
            return [None] * 8

        self.close(cm_trp, cm_qav, cm_s)
        self.close(cm_wq, cm_bc, cm_e, cm_m, cm_q, cm_p, cm_pt)
        self.close(cm_v, cm_k)

        wob, cm_wo = self.sb(None, [128, 2 * 512], BF16)
        bo_sb, cm_bo = self.sb(None, [1, C], F32)
        sgi, cm_si = self.sb(None, [128, 2 * 512], F32)
        sgo, cm_so = self.sb(None, [128, 2 * 512], F32)
        op, opcm = [], []
        for t in range(8):
            h_, c_ = self.ps([128, 512])
            op.append(h_)
            opcm.append(c_)
        t_bo = pl.dma("gp", "W", lambda e: e.dma_start(
            out=bo_sb[0:1, :], in_=self.BO), waits=[("SPE", t_bc)])
        wo_guard = {}
        evop = {}
        tt_tok = {}
        od_tok = {}
        for dch in range(4):
            for ci in range(16):
                idx = dch * 16 + ci
                par = idx % 2
                w = [("SPE", wo_guard[idx - 2])] if idx >= 2 else \
                    [("SPE", t_bc)]
                t_wo = pl.dma("gp", "W", lambda e, dch=dch, ci=ci, par=par:
                              e.dma_start(
                                  out=wob[:, par * 512:(par + 1) * 512],
                                  in_=self.WO[dch, ci]),
                              waits=w)
                for t in range(8):
                    w2 = [t_wo]
                    if ci == 0:
                        w2.append(("SDV", t_attn))
                        if dch > 0:
                            w2.append(("SDV", tt_tok[(dch - 1) * 8 + t]))
                    wo_guard[idx] = pl.emit(
                        "pe", lambda e, t=t, par=par, ci=ci: e.matmul(
                            op[t][:, :],
                            lhsT=attn_fm[:, ci * 1024 + t * 128:
                                         ci * 1024 + t * 128 + 128],
                            rhs=wob[:, par * 512:(par + 1) * 512],
                            start=(ci == 0), stop=False),
                        waits=w2, inc="SPE" if t == 7 else None)
            for t in range(8):
                k = dch * 8 + t
                par = k % 2
                isl = sgi[:, par * 512:(par + 1) * 512]
                osl = sgo[:, par * 512:(par + 1) * 512]
                pl.emit("pe", lambda e, t=t, dch=dch: e.matmul(
                    op[t][:, :], lhsT=self.ones_f[0:1, :],
                    rhs=bo_sb[0:1, dch * 512:(dch + 1) * 512],
                    start=False, stop=True),
                    waits=[t_bo])
                t_ch = pl.emit("pe", lambda e: e.drain(), inc="SPE")
                wrl = [("SDV", t_attn)]
                if k >= 2:
                    wrl.append(("SDV", tt_tok[k - 2]))
                t_rl = pl.dma("sy", "XR", lambda e, t=t, dch=dch, isl=isl:
                              e.dma_start(
                                  out=isl,
                                  in_=xsrc[t * 128:(t + 1) * 128,
                                           dch * 512:(dch + 1) * 512]),
                              waits=wrl)
                wtt = [("SPE", t_ch), t_rl]
                if k >= 2:
                    wtt.append(od_tok[k - 2])
                tt_tok[k] = pl.emit(
                    "dv", lambda e, t=t, isl=isl, osl=osl: e.tensor_tensor(
                        out=osl, in0=op[t][:, :], in1=isl, op=ALU.add),
                    waits=wtt, inc="SDV")
                od_tok[k] = pl.dma(
                    "sy", "XO", lambda e, t=t, dch=dch, osl=osl: e.dma_start(
                        out=xdst[t * 128:(t + 1) * 128,
                                 dch * 512:(dch + 1) * 512], in_=osl),
                    waits=[("SDV", tt_tok[k])])
        self.close(*reversed(opcm))
        self.close(cm_so, cm_si, cm_bo, cm_wo)
        self.close(cm_af)
        return [od_tok[3 * 8 + t] for t in range(8)]

    def build(self, upto="full"):
        pl = self.pl
        self.setup()
        if upto == "copy":
            x_tm, cm_x = self.sb(None, [128, TBLK * C], F32, side="right")
            for i in range(TBLK):
                t_ld = pl.dma("sy", "XL", lambda e, i=i: e.dma_start(
                    out=x_tm[:, i * C:(i + 1) * C],
                    in_=self.X[i * 128:(i + 1) * 128, :]))
                pl.dma("sy", "XO", lambda e, i=i: e.dma_start(
                    out=self.OUT[i * 128:(i + 1) * 128, :],
                    in_=x_tm[:, i * C:(i + 1) * C]), waits=[t_ld])
            self.close(cm_x)
        elif upto == "ln1":
            t_xn = self.emit_ln(self.X, [None] * 8)
            stg, cm_s = self.sb(None, [128, 2048], F32)
            for k in range(8):
                t_c = pl.emit("dv", lambda e, k=k: e.tensor_copy(
                    out=stg[:, :], in_=self.xn[:, k * 2048:(k + 1) * 2048]),
                    waits=[("SDV", t_xn)], inc="SDV")
                t_o = pl.dma("sy", "XO", lambda e, k=k: e.dma_start(
                    out=self.OUT[k * 128:(k + 1) * 128, :], in_=stg[:, :]),
                    waits=[("SDV", t_c)])
                pl.emit("dv", lambda e: e.engine_nop(),
                        waits=[t_o], inc="SDV")
            self.close(cm_s)
        elif upto == "sA":
            self.emit_ffn(self.X, [None] * 8, self.OUT, final=True,
                          stage_a_only=True)
        elif upto == "ffn1":
            self.emit_ffn(self.X, [None] * 8, self.OUT, final=True)
        elif upto == "attn":
            t1 = self.emit_ffn(self.X, [None] * 8, self.XSPA, final=False)
            self.emit_attention(self.XSPA, t1, self.OUT)
        else:
            t1 = self.emit_ffn(self.X, [None] * 8, self.XSPA, final=False)
            t2 = self.emit_attention(self.XSPA, t1, self.XSPB)
            self.emit_ffn(self.XSPB, t2, self.OUT, final=True)
        pl.emit("sy", lambda e: e.nop(),
                waits=pl.dma_barrier("XO") + pl.dma_barrier("XR") +
                pl.dma_barrier("XL") + pl.dma_barrier("W"))

        with self.st, self.nc.Block() as block:
            @block.sync
            def _(sync):
                pl.replay("sy", sync)

            @block.gpsimd
            def _(gp):
                pl.replay("gp", gp)

            @block.vector
            def _(dv):
                pl.replay("dv", dv)

            @block.scalar
            def _(ac):
                pl.replay("ac", ac)

            @block.tensor
            def _(pe):
                pl.replay("pe", pe)
        return self.nc



def kernel(L0_attn_x, L0_attn_Wq, L0_attn_Wkv, L0_attn_Wo, L0_attn_bo,
           L0_ffn_w_gate, L0_ffn_w_up, L0_ffn_w_down):
    x = np.ascontiguousarray(np.asarray(L0_attn_x, dtype=np.float32))
    Wq = np.asarray(L0_attn_Wq, np.float32)
    Wkv = np.asarray(L0_attn_Wkv, np.float32)
    Wo = np.asarray(L0_attn_Wo, np.float32)
    bo = np.asarray(L0_attn_bo, np.float32)
    wg = np.asarray(L0_ffn_w_gate, np.float32)
    wu = np.asarray(L0_ffn_w_up, np.float32)
    wd = np.asarray(L0_ffn_w_down, np.float32)

    WG = np.ascontiguousarray(wg.reshape(64, 128, 16, 128).transpose(0, 3, 2, 1))
    WU = np.ascontiguousarray(wu.reshape(64, 128, 16, 128).transpose(0, 3, 2, 1))
    WD = np.ascontiguousarray(wd.reshape(4, 512, 64, 128).transpose(0, 2, 3, 1))
    WQt = np.ascontiguousarray(Wq.reshape(16, 128, 16, 128).transpose(2, 1, 0, 3))
    WKVK = np.ascontiguousarray(
        Wkv[:, :512].reshape(16, 128, 4, 128).transpose(2, 1, 0, 3))
    WKVV = np.ascontiguousarray(Wkv[:, 512:].reshape(16, 128, 512))
    WOt = np.ascontiguousarray(Wo.reshape(16, 128, 4, 512).transpose(2, 0, 1, 3))
    BOr = np.ascontiguousarray(bo.reshape(1, C))

    slopes = _slopes()
    ii = np.arange(128)

    def mask_for(par):
        m = np.zeros((128, 8 * 256), np.float32)
        for q in range(8):
            g = QS[par][q]
            G = GMAX[q]
            for half, kb in enumerate((G - 1, G)):
                sl = m[:, q * 256 + half * 128: q * 256 + half * 128 + 128]
                if kb == g:
                    jj = kb * 128 + ii[None, :]
                    iglob = g * 128 + ii[:, None]
                    sl[:] = np.where(jj <= iglob, 0.0, -1e30)
                elif kb > g:
                    sl[:] = -1e30
        return m

    def evec_for(par):
        ev = np.zeros((128, H * 128), np.float64)
        for h in range(H):
            for q in range(8):
                g = QS[par][q]
                for kb in range(GMAX[q] + 1):
                    if kb > g:
                        continue
                    ev[:, h * 128 + q * 16 + kb] = np.exp(
                        slopes[h] * (ii - 64.0 - 128.0 * (g - kb)))
        return ev.astype(np.float32)

    masks = [mask_for(0), mask_for(1)]
    evecs = [evec_for(0), evec_for(1)]

    nc = Builder().build(_UPTO)
    in_maps = []
    for c in range(8):
        b, par = c // 2, c % 2
        xl = np.ascontiguousarray(
            x[b].reshape(NB, 128, C)[QS[par]].reshape(NTOK, C))
        in_maps.append({
            "X": xl, "WG": WG, "WU": WU, "WD": WD, "WQ": WQt,
            "WKVK": WKVK, "WKVV": WKVV, "WO": WOt, "BO": BOr,
            "MASKT": masks[par], "EVT": evecs[par],
        })
    res = run_bass_kernel_spmd(nc, in_maps, core_ids=list(range(8)))
    out = np.empty((B, T, C), np.float32)
    for c in range(8):
        b, par = c // 2, c % 2
        blk = np.asarray(res.results[c]["OUT"]).reshape(TBLK, 128, C)
        ob = out[b].reshape(NB, 128, C)
        for i, g in enumerate(QS[par]):
            ob[g] = blk[i]
    return out


if __name__ == "__main__":
    import jax
    with jax.default_device(jax.devices("cpu")[0]):
        import reference as ref
        inputs = ref.setup_inputs(0)
        d = {k: np.asarray(v) for k, v in inputs.items()}
        want = np.asarray(ref.reference(**inputs))
    got = kernel(**d)
    err = got - want
    rv = float((err ** 2).mean() / (want.astype(np.float64) ** 2).mean())
    print(f"residual-variance ratio = {rv:.3e}")
